# version 04
# baseline (speedup 1.0000x reference)
from contextlib import ExitStack
import numpy as np
import ml_dtypes
import concourse.bass as bass
import concourse.mybir as mybir
from concourse.bass_utils import run_bass_kernel_spmd

F32 = mybir.dt.float32
BF16 = mybir.dt.bfloat16
U32 = mybir.dt.uint32
AF = mybir.ActivationFunctionType
ALU = mybir.AluOpType
AX = mybir.AxisListType
BF = ml_dtypes.bfloat16

D = 2048
ND = 16
TOKC = 1024
NTC = 8
E = 64
T = 4096
NH = 4
EPS = 1e-5

PHASES = ("ffn1", "attn", "ffn2")
DBG = {"n_exp": E, "router": True, "shared": True, "bcast": True, "dn": True, "pool_eng": "dve", "stream": 1, "dump": False}


class KB:
    ENG = {"pe": "tensor", "dve": "vector", "act": "scalar", "pool": "gpsimd",
           "sp": "sync"}

    def __init__(self, nc):
        self.nc = nc
        self.ops = {k: [] for k in self.ENG}
        self.cnt = {}
        self.sems = {}
        self.blk = 0
        self.dn_after = []
        self.pe_carry = []
        self.crow_i = 0
        self.piece_i = 0
        self.half_i = 0
        self.exp_i = 0

    def sem(self, name):
        if name not in self.sems:
            self.sems[name] = self.nc.alloc_semaphore(name)
            self.cnt[name] = 0
        return self.sems[name]

    def n(self, name):
        self.sem(name)
        return self.cnt[name]

    def carry_pe(self, name, k=1):
        self.sem(name)
        self.cnt[name] += k
        self.pe_carry.append((self.sems[name], k))

    def do(self, eng, meth, *args, waits=(), incs=(), **kwargs):
        whs = [(self.sem(nm), v) for nm, v in waits if v > 0]
        ihs = []
        for nm, k in incs:
            assert k > 0, (nm, k)
            self.sem(nm)
            self.cnt[nm] += k
            ihs.append((self.sems[nm], k))
        if eng == "pe":
            assert len(ihs) <= 1, meth
            if not ihs and self.pe_carry:
                ihs.append(self.pe_carry.pop(0))

        def fn(EE):
            for h, v in whs:
                EE.wait_ge(h, v)
            r = getattr(EE, meth)(*args, **kwargs)
            for h, k in ihs:
                r.then_inc(h, k)
        self.ops[eng].append(fn)

    def wait(self, eng, name, val):
        if val <= 0:
            return
        h = self.sem(name)
        self.ops[eng].append(lambda EE: EE.wait_ge(h, val))

    def coll(self, kind, rg, src, dst, waits=(), incs=()):
        whs = [(self.sem(nm), v) for nm, v in waits if v > 0]
        ihs = []
        for nm, k in incs:
            self.sem(nm)
            self.cnt[nm] += k
            ihs.append(self.sems[nm])

        def fn(EE):
            for h, v in whs:
                EE.wait_ge(h, v)
            r = EE.collective_compute(kind, ALU.bypass, replica_groups=rg,
                                      ins=[src.opt()], outs=[dst.opt()])
            for h in ihs:
                r.then_inc(h)
        self.ops["pool"].append(fn)

    def flush(self, name=None):
        ops = {k: v[:] for k, v in self.ops.items() if v}
        for k in self.ops:
            self.ops[k] = []
        self.blk += 1
        if not ops:
            return
        with self.nc.Block(name or f"blk{self.blk}") as b:
            for engname, lst in ops.items():
                def make(lst):
                    def body(EE):
                        for fn in lst:
                            fn(EE)
                    return body
                getattr(b, self.ENG[engname])(make(lst))


def emit_ln(kb, bufs):
    xs3, hT, h_bf = bufs["xs3"], bufs["hT"], bufs["h_bf"]
    bnb, mvb, rstd = bufs["bnb"], bufs["mvb"], bufs["rstd"]
    ident_bf, ps_tr = bufs["ident_bf"], bufs["ps_tr"]
    hT3 = hT.rearrange("p (d t) -> p d t", t=TOKC)
    for tc in range(NTC):
        for ch in range(4):
            kb.do("dve", "bn_stats", bnb[:, ch * 6:(ch + 1) * 6],
                  xs3[:, tc, ch * 512:(ch + 1) * 512],
                  waits=([("s_lnb", 16)] if ch == 0 else []),
                  incs=([("s_ln", 1)] if ch == 3 else []))
        kb.do("dve", "bn_aggr", mvb[:], bnb[:],
              waits=[("s_ln", kb.n("s_ln"))], incs=[("s_ln", 1)])
        kb.do("act", "activation", rstd[:, 0:1], mvb[:, 1:2], AF.Sqrt, bias=EPS,
              waits=[("s_ln", kb.n("s_ln"))], incs=[("s_ln", 1)])
        kb.do("dve", "reciprocal", rstd[:, 1:2], rstd[:, 0:1],
              waits=[("s_ln", kb.n("s_ln"))], incs=[("s_ln", 1)])
        kb.do("dve", "tensor_scalar", h_bf[:], xs3[:, tc, :], mvb[:, 0:1],
              rstd[:, 1:2], op0=ALU.subtract, op1=ALU.mult,
              waits=[("s_ln", kb.n("s_ln")), ("s_trp", kb.n("s_trp"))],
              incs=[("s_ln", 1)])
        for grp in range(4):
            pp = (tc * 4 + grp) % 2
            for j in range(4):
                d = grp * 4 + j
                w = []
                if j == 0:
                    w = [("s_ln", kb.n("s_ln")),
                         ("s_hT", max(0, kb.n("s_hT") - 1))]
                    if tc == 0 and grp < 2:
                        w.append(("s_ev", kb.n("s_ev")))
                        w.append(("s_woe", kb.n("s_woe")))
                kb.do("pe", "transpose", ps_tr[pp][:, j * 128:(j + 1) * 128],
                      h_bf[:, d * 128:(d + 1) * 128], ident_bf[:],
                      waits=w, incs=([("s_trp", 1)] if j == 3 else []))
            kb.do("dve", "tensor_copy",
                  hT3[:, grp * 4:(grp + 1) * 4, tc * 128:(tc + 1) * 128],
                  ps_tr[pp].rearrange("p (j t) -> p j t", t=128),
                  waits=[("s_trp", kb.n("s_trp"))], incs=[("s_hT", 1)])


def emit_ffn(kb, bufs, dram, call):
    NE = DBG["n_exp"]
    do_router = DBG["router"]
    xs3 = bufs["xs3"]
    hT3 = bufs["hT"].rearrange("p (d t) -> p d t", t=TOKC)
    ident_f32 = bufs["ident_f32"]
    gu_ps, dn_ps = bufs["gu_ps"], bufs["dn_ps"]
    ps32 = gu_ps[1][:, 0:512]
    ps_r = gu_ps[0][0:64, :]
    ps_g = gu_ps[0][0:1, :]
    ps_gc = gu_ps[0][:, 0:1024]

    emit_ln(kb, bufs)
    hT_done = kb.n("s_hT")

    wr_sb = bufs["wr_sb"].rearrange("p (d e) -> p d e", e=64)
    if do_router:
        gate_guard = [("s_sg", kb.n("s_sg")), ("s_up", kb.n("s_up")),
                      ("s_lgtm", kb.n("s_lgtm")), ("s_cT", kb.n("s_cT")),
                      ("s_gce", kb.n("s_gce")), ("s_shp", kb.n("s_shp")),
                      ("s_shs", kb.n("s_shs")), ("s_cbcE", kb.n("s_cbcE"))]
        for d in range(ND):
            for nf in range(2):
                kb.do("pe", "matmul", ps_r[:, nf * 512:(nf + 1) * 512],
                      wr_sb[:, d, :], hT3[:, d, nf * 512:(nf + 1) * 512],
                      start=(d == 0), stop=(d == ND - 1),
                      waits=([("s_ldp", 32), ("s_hT", hT_done)] + gate_guard
                             if d == 0 and nf == 0 else []),
                      incs=([("s_rps", 1)] if d == ND - 1 and nf == 1 else []))
        lgT = bufs["lgT"]
        kb.do("dve", "tensor_copy", lgT[:], ps_r[:],
              waits=[("s_rps", kb.n("s_rps"))], incs=[("s_lgT", 1)])
        lg_tm = bufs["lg_tm"].rearrange("p (tc e) -> p tc e", e=64)
        for half in range(2):
            for tcq in range(4):
                tc = half * 4 + tcq
                kb.do("pe", "matmul", ps32[:, tcq * 64:(tcq + 1) * 64],
                      lgT[:, tc * 128:(tc + 1) * 128], ident_f32[0:64, 0:64],
                      start=True, stop=True,
                      waits=([("s_lgT", kb.n("s_lgT")),
                              ("s_lgtm", kb.n("s_lgtm"))] if tcq == 0 else []),
                      incs=([("s_lgps", 1)] if tcq == 3 else []))
            kb.do("dve", "tensor_copy",
                  lg_tm[:, half * 4:(half + 1) * 4, :],
                  ps32.rearrange("p (tcq e) -> p tcq e", e=64)[:, 0:4, :],
                  waits=[("s_lgps", kb.n("s_lgps"))], incs=[("s_lgtm", 1)])

        probs = bufs["probs"].rearrange("p (tc e) -> p tc e", e=64)
        repl, mx8, cl1 = bufs["repl"], bufs["mx8"], bufs["cl1"]
        comb_tm = bufs["comb_tm"].rearrange("p (tc e) -> p tc e", e=64)
        for tc in range(NTC):
            kb.do("dve", "reduce_max", cl1[:, 0:1], lg_tm[:, tc, :], axis=AX.X,
                  waits=[("s_lgtm", kb.n("s_lgtm"))], incs=[("s_sm", 1)])
            kb.do("dve", "tensor_scalar", cl1[:, 1:2], cl1[:, 0:1], -1.0, None,
                  op0=ALU.mult, waits=[("s_sm", kb.n("s_sm"))], incs=[("s_sm", 1)])
            kb.do("act", "activation", probs[:, tc, :], lg_tm[:, tc, :], AF.Exp,
                  bias=cl1[:, 1:2],
                  waits=[("s_sm", kb.n("s_sm"))], incs=[("s_sm", 1)])
            kb.do("dve", "reduce_sum", cl1[:, 2:3], probs[:, tc, :], axis=AX.X,
                  waits=[("s_sm", kb.n("s_sm"))], incs=[("s_sm", 1)])
            kb.do("dve", "reciprocal", cl1[:, 3:4], cl1[:, 2:3],
                  waits=[("s_sm", kb.n("s_sm"))], incs=[("s_sm", 1)])
            kb.do("dve", "tensor_scalar", probs[:, tc, :], probs[:, tc, :],
                  cl1[:, 3:4], None, op0=ALU.mult,
                  waits=[("s_sm", kb.n("s_sm"))], incs=[("s_sm", 1)])
            kb.do("dve", "max", out=mx8[:], in_=probs[:, tc, :],
                  waits=[("s_sm", kb.n("s_sm"))], incs=[("s_sm", 1)])
            kb.do("dve", "match_replace", out=repl[:], in_to_replace=mx8[:],
                  in_values=probs[:, tc, :], imm_value=0.0,
                  waits=[("s_sm", kb.n("s_sm"))], incs=[("s_sm", 1)])
            kb.do("dve", "tensor_tensor", repl[:], probs[:, tc, :], repl[:],
                  op=ALU.subtract,
                  waits=[("s_sm", kb.n("s_sm"))], incs=[("s_sm", 1)])
            kb.do("dve", "reduce_sum", cl1[:, 0:1], repl[:], axis=AX.X,
                  waits=[("s_sm", kb.n("s_sm"))], incs=[("s_sm", 1)])
            kb.do("dve", "reciprocal", cl1[:, 1:2], cl1[:, 0:1],
                  waits=[("s_sm", kb.n("s_sm"))], incs=[("s_sm", 1)])
            kb.do("dve", "tensor_scalar", comb_tm[:, tc, :], repl[:], cl1[:, 1:2],
                  0.5, op0=ALU.mult, op1=ALU.mult,
                  waits=[("s_sm", kb.n("s_sm"))], incs=[("s_sm", 1)])
        combT = bufs["combT"]
        for half in range(2):
            for tcq in range(4):
                tc = half * 4 + tcq
                kb.do("pe", "matmul", gu_ps[1][0:64, tcq * 128:(tcq + 1) * 128],
                      comb_tm[:, tc, :], ident_f32[:],
                      start=True, stop=True,
                      waits=([("s_sm", kb.n("s_sm")), ("s_lgtm", kb.n("s_lgtm")),
                              ("s_cT", kb.n("s_cT"))] if tcq == 0 else []),
                      incs=([("s_cps", 1)] if tcq == 3 else []))
            kb.do("dve", "tensor_copy",
                  combT.rearrange("e (hh t) -> e hh t", t=512)[:, half, :],
                  gu_ps[1][0:64, 0:512],
                  waits=[("s_cps", kb.n("s_cps"))], incs=[("s_cT", 1)])
    combT_done = kb.n("s_cT")

    wgu_pc, wd_pc, aT = bufs["wgu_pc"], bufs["wd_pc"], bufs["aT"]
    sgb, upb = bufs["sgb"], bufs["upb"]
    comb_bc, crow = bufs["comb_bc"], bufs["crow"]
    ones_row = bufs["ones_row"]
    wguT_d, wdT_d = dram["wguT"], dram["wdT"]
    NWD = len(wd_pc)

    def emit_crow(e):
        if not DBG["bcast"]:
            return
        ci = kb.crow_i
        kb.crow_i += 1
        w = [("s_cT", combT_done)]
        if ci >= 2:
            w.append(("s_cbcMM", ci - 1))
        kb.do("pool", "dma_start", crow[ci % 2][:], combT[e:e + 1, :], waits=w,
              incs=[(f"s_crow{ci % 2}", 16)])

    def emit_pieces(e, pairs):
        if DBG["stream"] in (0, 2):
            if e == 0 and pairs[0] == 0 and call == 0:
                kb.do("dve", "memset", wgu_pc[0][:], 0.25, incs=[("s_wgu0", 16352)])
                kb.do("dve", "memset", wgu_pc[1][:], 0.25, incs=[("s_wgu1", 16352)])
            return
        for pair in pairs:
            k = kb.piece_i
            kb.piece_i += 1
            w = [("s_guC", k - 1)] if k >= 2 else []
            src = wguT_d[e].rearrange("(c p) o -> p c o", p=128)
            dst = wgu_pc[k % 2].rearrange("p (c o) -> p c o", o=256)
            kb.do("pool", "dma_start", dst[:, :, 0:128],
                  src[:, :, pair * 128:(pair + 1) * 128], waits=w,
                  incs=[(f"s_wgu{k % 2}", 16)])
            kb.do("pool", "dma_start", dst[:, :, 128:256],
                  src[:, :, 512 + pair * 128:512 + (pair + 1) * 128],
                  incs=[(f"s_wgu{k % 2}", 16)])

    def emit_halves(e):
        if DBG["stream"] in (0, 3):
            if e == 0 and call == 0:
                for i in range(len(wd_pc)):
                    kb.do("dve", "memset", wd_pc[i][:], 0.25, incs=[(f"s_wd{i}", 8192)])
            return
        for oh in range(2):
            k = kb.half_i
            kb.half_i += 1
            w = [("s_wdC", k - (NWD - 1))] if k >= NWD else []
            src = wdT_d[e].rearrange("(ic p) o -> p ic o", p=128)
            kb.do("pool", "dma_start",
                  wd_pc[k % NWD].rearrange("p (ic o) -> p ic o", o=1024),
                  src[:, :, oh * 1024:(oh + 1) * 1024], waits=w,
                  incs=[(f"s_wd{k % NWD}", 16)])

    if NE > 0:
        emit_pieces(0, [0, 1])
        emit_crow(0)
        emit_pieces(0, [2, 3])
        emit_halves(0)
        for e in range(1, NE):
            emit_crow(e)
            emit_pieces(e, [0, 1, 2, 3])
            emit_halves(e)

    base_a = kb.n("s_a")
    base_cbc = kb.n("s_cbcE")
    gu_slot = kb.n("s_gu")
    dn_slot = kb.n("s_dn")

    for e in range(NE):
        ke = kb.exp_i
        if DBG["bcast"]:
            w = [(f"s_crow{ke % 2}", (ke // 2 + 1) * 16)]
            if gu_slot >= 1:
                w += [("s_sg", gu_slot), ("s_up", gu_slot)]
            if e == 0:
                w += [("s_lgtm", kb.n("s_lgtm")), ("s_cT", combT_done)]
            kb.do("pe", "matmul", gu_ps[1][:, 0:512], ones_row[:],
                  crow[ke % 2][0:1, 0:512], start=True, stop=True, waits=w)
            kb.do("pe", "matmul", gu_ps[1][:, 512:1024], ones_row[:],
                  crow[ke % 2][0:1, 512:1024], start=True, stop=True,
                  incs=[("s_cbcMM", 1)])
            kb.do("dve", "tensor_copy", comb_bc[ke % 2][:], gu_ps[1][:],
                  waits=[("s_cbcMM", kb.n("s_cbcMM"))], incs=[("s_cbcE", 1)])
        else:
            kb.do("dve", "memset", comb_bc[ke % 2][:], 0.5,
                  incs=[("s_cbcE", 1)])

        for pair in range(4):
            for th in range(2):
                k = gu_slot
                pp = k % 2
                piece = ke * 4 + pair
                w = [(f"s_wgu{piece % 2}", (piece // 2 + 1) * 32)]
                if k >= 2:
                    w += [("s_sg", k - 1), ("s_up", k - 1)]
                if pp == 1 and DBG["bcast"]:
                    w.append(("s_cbcE", kb.n("s_cbcE")))
                wp = wgu_pc[piece % 2].rearrange("p (c o) -> p c o", o=256)
                for gi in range(2):
                    for d in range(ND):
                        last = (gi == 1 and d == ND - 1)
                        kb.do("pe", "matmul",
                              gu_ps[pp][:, gi * 512:(gi + 1) * 512],
                              wp[:, d, gi * 128:(gi + 1) * 128],
                              hT3[:, d, th * 512:(th + 1) * 512],
                              start=(d == 0), stop=(d == ND - 1),
                              waits=w if (gi == 0 and d == 0) else [],
                              incs=([("s_gu", 1)] if last else []))
                if th == 1:
                    kb.carry_pe("s_guC", 1)
                gu_slot += 1
                kb.do("act", "activation", sgb[k % 2][:], gu_ps[pp][:, 0:512],
                      AF.Silu,
                      waits=[("s_gu", k + 1)] + ([("s_a", k - 1)] if k >= 2 else []),
                      incs=[("s_sg", 1)])
                kb.do("dve", "tensor_tensor", upb[k % 2][:],
                      gu_ps[pp][:, 512:1024],
                      comb_bc[ke % 2][:, th * 512:(th + 1) * 512], op=ALU.mult,
                      waits=[("s_gu", k + 1), ("s_cbcE", base_cbc + e + 1)] +
                            ([("s_a", k - 1)] if k >= 2 else []),
                      incs=[("s_up", 1)])
                wpool = [("s_sg", k + 1), ("s_up", k + 1)]
                if len(kb.dn_after) >= 2:
                    wpool.append(("s_dn", kb.dn_after[-2]))
                if e == 0:
                    wpool.append(("s_cps", kb.n("s_cps")))
                kb.do(DBG["pool_eng"], "tensor_tensor",
                      aT[ke % 2][:, pair * 1024 + th * 512:
                                 pair * 1024 + (th + 1) * 512],
                      sgb[k % 2][:], upb[k % 2][:], op=ALU.mult,
                      waits=wpool, incs=[("s_a", 1)])
        if DBG["dn"]:
            aTe = aT[ke % 2].rearrange("p (ic t) -> p ic t", t=1024)
            for tc in range(NTC):
                for oh in range(2):
                    j = dn_slot
                    dp = j % 2
                    half = ke * 2 + oh
                    w = [(f"s_wd{half % NWD}", (half // NWD + 1) * 16),
                     ("s_a", base_a + 8 * (e + 1))]
                    if j >= 2:
                        w.append(("s_ev", j - 1))
                    if e == 0 and tc == 0 and oh == 0:
                        w.append(("s_hT", hT_done))
                    wdp = wd_pc[half % NWD].rearrange("p (ic o) -> p ic o", o=1024)
                    for ic in range(4):
                        for on in range(2):
                            last = (ic == 3 and on == 1)
                            kb.do("pe", "matmul",
                                  dn_ps[dp][:, on * 512:(on + 1) * 512],
                                  aTe[:, ic, tc * 128:(tc + 1) * 128],
                                  wdp[:, ic, on * 512:(on + 1) * 512],
                                  start=(ic == 0), stop=(ic == 3),
                                  waits=w if (ic == 0 and on == 0) else [],
                                  incs=([("s_dn", 1)] if last else []))
                    if tc == NTC - 1:
                        kb.carry_pe("s_wdC", 1)
                    dn_slot += 1
                    kb.do("dve", "tensor_tensor",
                          xs3[:, tc, oh * 1024:(oh + 1) * 1024],
                          xs3[:, tc, oh * 1024:(oh + 1) * 1024], dn_ps[dp][:],
                          op=ALU.add,
                          waits=[("s_dn", j + 1)], incs=[("s_ev", 1)])
        kb.dn_after.append(dn_slot)
        kb.exp_i += 1

    if not DBG["shared"]:
        return
    wsg_sb, wsu_sb, wsd_sb = bufs["wsg_sb"], bufs["wsu_sb"], bufs["wsd_sb"]
    wseg_sb = bufs["wseg_sb"]
    kb.do("pool", "dma_start", wsg_sb,
          dram["wsgT"].rearrange("(c p) o -> p c o", p=128),
          waits=[("s_guC", kb.n("s_guC")), ("s_wdC", kb.n("s_wdC")),
                 ("s_a", kb.n("s_a")), ("s_dn", kb.n("s_dn"))],
          incs=[("s_wsh", 16)])
    kb.do("pool", "dma_start", wsu_sb,
          dram["wsuT"].rearrange("(c p) o -> p c o", p=128),
          incs=[("s_wsh", 16)])
    kb.do("pool", "dma_start", wsd_sb,
          dram["wsdT"].rearrange("(ic p) o -> p ic o", p=128),
          incs=[("s_wsh", 16)])
    wsh_done = kb.n("s_wsh")

    ps_guard = [("s_sg", kb.n("s_sg")), ("s_up", kb.n("s_up")),
                ("s_cbcE", kb.n("s_cbcE")), ("s_hT", kb.n("s_hT"))]
    gate_col = bufs["gate_col"]
    wseg8 = wseg_sb.rearrange("p (d j) -> p d j", j=8)
    for tc in range(NTC):
        w = list(ps_guard) + [("s_ldp", 32)] if tc == 0 else []
        if tc >= 2:
            w = [("s_gr", kb.n("s_gr") - 1)]
        for d in range(ND):
            kb.do("pe", "matmul", ps_gc[:, (tc % 2) * 512:(tc % 2) * 512 + 8],
                  hT3[:, d, tc * 128:(tc + 1) * 128], wseg8[:, d, :],
                  start=(d == 0), stop=(d == ND - 1),
                  waits=w if d == 0 else [],
                  incs=([("s_gr", 1)] if d == ND - 1 else []))
        kb.do("act", "activation", bufs["gate_raw"][:, tc:tc + 1],
              ps_gc[:, (tc % 2) * 512:(tc % 2) * 512 + 1], AF.Sigmoid,
              waits=[("s_gr", kb.n("s_gr"))], incs=[("s_gr", 1)])
    kb.do("dve", "tensor_scalar", gate_col[:], bufs["gate_raw"][:], 0.5, None,
          op0=ALU.mult, waits=[("s_gr", kb.n("s_gr"))], incs=[("s_gce", 1)])

    s_bf = bufs["s_bf"].rearrange("p (ic t) -> p ic t", t=1024)
    shsg = bufs["shsg"]
    for ofc in range(4):
        wg = [("s_wsh", wsh_done), ("s_gce", kb.n("s_gce")),
              ("s_hT", kb.n("s_hT"))]
        if ofc == 0:
            wg += [("s_a", kb.n("s_a")), ("s_dn", kb.n("s_dn"))]
        else:
            wg += [("s_shs", kb.n("s_shs"))]
        for d in range(ND):
            for nf in range(2):
                kb.do("pe", "matmul", gu_ps[0][:, nf * 512:(nf + 1) * 512],
                      wsg_sb[:, d, ofc * 128:(ofc + 1) * 128],
                      hT3[:, d, nf * 512:(nf + 1) * 512],
                      start=(d == 0), stop=(d == ND - 1),
                      waits=wg if (d == 0 and nf == 0) else [],
                      incs=([("s_shg", 1)] if d == ND - 1 and nf == 1 else []))
        wu = [("s_shp", kb.n("s_shp"))] if ofc > 0 else []
        for d in range(ND):
            for nf in range(2):
                kb.do("pe", "matmul", gu_ps[1][:, nf * 512:(nf + 1) * 512],
                      wsu_sb[:, d, ofc * 128:(ofc + 1) * 128],
                      hT3[:, d, nf * 512:(nf + 1) * 512],
                      start=(d == 0), stop=(d == ND - 1),
                      waits=wu if (d == 0 and nf == 0) else [],
                      incs=([("s_shu", 1)] if d == ND - 1 and nf == 1 else []))
        kb.do("act", "activation", shsg[:], gu_ps[0][:], AF.Silu,
              waits=[("s_shg", kb.n("s_shg"))] +
                    ([("s_shp", kb.n("s_shp"))] if ofc > 0 else []),
              incs=[("s_shs", 1)])
        kb.do("dve", "tensor_tensor", s_bf[:, ofc, :], shsg[:], gu_ps[1][:],
              op=ALU.mult,
              waits=[("s_shu", kb.n("s_shu")), ("s_shs", kb.n("s_shs"))],
              incs=[("s_shp", 1)])
    shp_done = kb.n("s_shp")

    for tc in range(NTC):
        j = dn_slot
        dp = j % 2
        w = [("s_shp", shp_done), ("s_gce", kb.n("s_gce")),
             ("s_ev", kb.n("s_ev"))]
        for half in range(2):
            tgt = dn_ps[(dp + half) % 2]
            for ic in range(4):
                for on in range(2):
                    last = (half == 1 and ic == 3 and on == 1)
                    kb.do("pe", "matmul", tgt[:, on * 512:(on + 1) * 512],
                          s_bf[:, ic, tc * 128:(tc + 1) * 128],
                          wsd_sb[:, ic, half * 1024 + on * 512:
                                 half * 1024 + (on + 1) * 512],
                          start=(ic == 0), stop=(ic == 3),
                          waits=w if (half == 0 and ic == 0 and on == 0) else [],
                          incs=([("s_dn", 2)] if last else []))
        dn_slot += 2
        for half in range(2):
            kb.do("dve", "scalar_tensor_tensor",
                  xs3[:, tc, half * 1024:(half + 1) * 1024],
                  dn_ps[(dp + half) % 2][:], gate_col[:, tc:tc + 1],
                  xs3[:, tc, half * 1024:(half + 1) * 1024],
                  op0=ALU.mult, op1=ALU.add,
                  waits=[("s_dn", j + 2)], incs=[("s_ev", 1)])



def mk_ffn_bufs(nc, es, g, tag):
    bufs = dict(g)
    arena = es.enter_context(nc.sbuf_tensor(f"arena{tag}", [128, 7 * 4096], BF16)).ap()
    bufs["arena_raw"] = arena
    bufs["wgu_pc"] = [arena[:, i * 4096:(i + 1) * 4096] for i in range(2)]
    bufs["wd_pc"] = [arena[:, (2 + i) * 4096:(3 + i) * 4096] for i in range(3)]
    bufs["aT"] = [arena[:, 6 * 4096:7 * 4096], arena[:, 5 * 4096:6 * 4096]]
    bufs["s_bf"] = arena[:, 6 * 4096:7 * 4096]
    bufs["wsg_sb"] = arena[:, 0:8192].rearrange("p (d o) -> p d o", o=512)
    bufs["wsu_sb"] = arena[:, 8192:16384].rearrange("p (d o) -> p d o", o=512)
    bufs["wsd_sb"] = arena[:, 16384:24576].rearrange("p (ic o) -> p ic o", o=2048)
    ar32 = arena.bitcast(F32)
    bufs["lg_tm"] = ar32[:, 10240:10752]
    bufs["probs"] = ar32[:, 10752:11264]
    bufs["comb_tm"] = ar32[:, 11264:11776]
    bufs["lgT"] = ar32[0:64, 11776:12800]
    for nm, shape, dt in [("repl", [128, 64], F32),
                          ("mx8", [128, 8], F32), ("cl1", [128, 4], F32),
                          ("combT", [64, 1024], F32),
                          ("epil", [128, 2048], F32),
                          ("cbc0", [128, 1024], F32), ("cbc1", [128, 1024], F32),
                          ("crow0", [1, 1024], F32), ("crow1", [1, 1024], F32),
                          ("gate_raw", [128, 8], F32),
                          ("gate_col", [128, 8], F32)]:
        bufs[nm] = es.enter_context(nc.sbuf_tensor(f"f{tag}_{nm}", shape, dt)).ap()
    ep = bufs["epil"]
    bufs["sgb"] = [ep[:, 0:512], ep[:, 512:1024]]
    bufs["upb"] = [ep[:, 1024:1536], ep[:, 1536:2048]]
    bufs["shsg"] = ep[:, 0:1024]
    bufs["comb_bc"] = [bufs["cbc0"], bufs["cbc1"]]
    bufs["crow"] = [bufs["crow0"], bufs["crow1"]]
    bufs["gu_ps"] = [es.enter_context(nc.psum_tensor(f"gu{tag}{i}", [128, 1024], F32)).ap()
                     for i in range(2)]
    bufs["dn_ps"] = [es.enter_context(nc.psum_tensor(f"dn{tag}{i}", [128, 1024], F32)).ap()
                     for i in range(2)]
    bufs["ps_tr"] = [bufs["dn_ps"][i].bitcast(BF16)[:, 0:512] for i in range(2)]
    return bufs


def build_nc(phases=PHASES):
    nc = bass.Bass("TRN2", target_bir_lowering=False)
    kb = KB(nc)
    groups4 = [[0, 1, 2, 3], [4, 5, 6, 7]]

    dram = {}
    for nm, shape in [("x_in", [TOKC, D]), ("wguT", [E, D, 1024]),
                      ("wdT", [E, 512, D]), ("wrT", [D, E]),
                      ("wsgT", [D, 512]), ("wsuT", [D, 512]),
                      ("wsdT", [512, D]), ("wseg8", [128, 128]),
                      ("wqkvT", [D, 1536]), ("woT", [D, D]),
                      ("gbq", [2, 128, 512]), ("ident_f32", [128, 128]),
                      ("ones_row", [1, 128]), ("onesone", [1, 1])]:
        dram[nm] = nc.dram_tensor(nm, shape, F32, kind="ExternalInput").ap()
    for nm, shape in [("ident_bf", [128, 128]), ("masks", [4, 128, 512]),
                      ("ones_bf", [128, 1])]:
        dram[nm] = nc.dram_tensor(nm, shape, BF16, kind="ExternalInput").ap()
    dram["gthidx"] = nc.dram_tensor("gthidx", [128, 16], U32, kind="ExternalInput").ap()
    dram["fconst"] = nc.dram_tensor("fconst", [128, 2], F32, kind="ExternalInput").ap()
    dram["x_out"] = nc.dram_tensor("x_out", [TOKC, D], F32, kind="ExternalOutput").ap()
    dram["xspill"] = nc.dram_tensor("xspill", [TOKC, D], F32, kind="Internal").ap()
    dram["ag1_in"] = nc.dram_tensor("ag1_in", [D, TOKC], BF16, kind="Internal").ap()
    dram["ag1_out"] = nc.dram_tensor("ag1_out", [4 * D, TOKC], BF16, kind="Internal").ap()
    dram["ag2_in"] = nc.dram_tensor("ag2_in", [D, TOKC], BF16, kind="Internal").ap()
    dram["ag2_out"] = nc.dram_tensor("ag2_out", [4 * D, TOKC], BF16, kind="Internal").ap()

    g = {}
    g["ident_bf"] = nc.alloc_sbuf_tensor("c_idb", [128, 128], BF16).ap()
    g["ident_f32"] = nc.alloc_sbuf_tensor("c_idf", [128, 128], F32).ap()
    g["masks"] = nc.alloc_sbuf_tensor("c_msk", [128, 4 * 512], BF16).ap()
    g["ones_bf"] = nc.alloc_sbuf_tensor("c_ones", [128, 1], BF16).ap()
    g["ones_row"] = nc.alloc_sbuf_tensor("c_onr", [1, 128], F32).ap()
    g["onesone"] = nc.alloc_sbuf_tensor("c_one1", [1, 1], F32).ap()
    g["gbq"] = nc.alloc_sbuf_tensor("c_gbq", [128, 2 * 512], F32).ap()
    g["gthidx"] = nc.alloc_sbuf_tensor("c_gidx", [128, 16], U32).ap()
    g["wseg_sb"] = nc.alloc_sbuf_tensor("wseg", [128, 128], BF16).ap()
    g["wr_sb"] = nc.alloc_sbuf_tensor("wr", [128, 16 * 64], BF16).ap()
    g["h_bf"] = nc.alloc_sbuf_tensor("h_bf", [128, D], BF16).ap()
    g["bnb"] = nc.alloc_sbuf_tensor("bnb", [128, 24], F32).ap()
    g["mvb"] = nc.alloc_sbuf_tensor("mvb", [128, 2], F32).ap()
    g["rstd"] = nc.alloc_sbuf_tensor("rstd", [128, 2], F32).ap()
    g["fconst"] = nc.alloc_sbuf_tensor("c_fc", [128, 2], F32).ap()
    nc.const_aps.register(F32, EPS, g["fconst"][:, 0:1])
    nc.const_aps.register(F32, 128 * EPS, g["fconst"][:, 1:2])

    for nm, dst in [("ident_bf", g["ident_bf"]), ("ident_f32", g["ident_f32"]),
                    ("ones_bf", g["ones_bf"]), ("ones_row", g["ones_row"]),
                    ("onesone", g["onesone"]), ("gthidx", g["gthidx"])]:
        kb.do("sp", "dma_start", dst, dram[nm][:], incs=[("s_ld", 16)])
    kb.do("sp", "dma_start", g["masks"].rearrange("p (m f) -> p m f", f=512),
          dram["masks"].rearrange("m p f -> p m f"), incs=[("s_ld", 16)])
    kb.do("sp", "dma_start", g["gbq"].rearrange("p (a f) -> p a f", f=512),
          dram["gbq"].rearrange("a p f -> p a f"), incs=[("s_ld", 16)])
    kb.do("sp", "dma_start", g["fconst"], dram["fconst"][:], incs=[("s_ld", 16)])
    kb.do("pool", "dma_start", g["wr_sb"].rearrange("p (d e) -> p d e", e=64),
          dram["wrT"].rearrange("(d p) e -> p d e", p=128), incs=[("s_ldp", 16)])
    kb.do("pool", "dma_start", g["wseg_sb"], dram["wseg8"][:],
          incs=[("s_ldp", 16)])

    kb.wait("pe", "s_ld", kb.n("s_ld"))
    kb.wait("act", "s_ld", kb.n("s_ld"))
    kb.wait("dve", "s_ld", kb.n("s_ld"))

    es1 = ExitStack()
    xs1 = es1.enter_context(nc.sbuf_tensor("xs1", [128, NTC * D], F32)).ap()
    hT1 = es1.enter_context(nc.sbuf_tensor("hT1", [128, ND * TOKC], BF16)).ap()
    xs1v = xs1.rearrange("p (tc d) -> p tc d", d=D)
    kb.do("sp", "dma_start", xs1v,
          dram["x_in"].rearrange("(tc p) d -> p tc d", p=128),
          incs=[("s_lnb", 16)])
    b1 = mk_ffn_bufs(nc, es1, g, "A")
    b1["xs3"] = xs1v
    b1["hT"] = hT1

    if "ffn1" in phases:
        emit_ffn(kb, b1, dram, call=0)

    if "attn" in phases:
        emit_ln(kb, b1)
        kb.wait("sp", "s_hT", kb.n("s_hT"))
        kb.do("sp", "dma_start",
              dram["ag1_in"].rearrange("(d p) t -> p d t", p=128),
              hT1.rearrange("p (d t) -> p d t", t=TOKC),
              incs=[("s_ag1st", 16)])
        kb.wait("sp", "s_ev", kb.n("s_ev"))
        kb.wait("sp", "s_lnb", 16)
        kb.do("sp", "dma_start",
              dram["xspill"].rearrange("(tc p) d -> p tc d", p=128), xs1v,
              incs=[("s_spill", 16)])
        kb.coll("AllGather", groups4, dram["ag1_in"], dram["ag1_out"],
                waits=[("s_ag1st", 16)], incs=[("s_cc1", 1)])
        kb.flush("ffn1_ln2")
        es1.close()
        emit_attn(kb, g, dram)
    else:
        kb.flush("ffn1_only")

    if "ffn2" in phases:
        es2 = ExitStack()
        if "attn" in phases:
            xs2v = kb._xs_wo
        else:
            xs2v = xs1v
        hT2 = es2.enter_context(nc.sbuf_tensor("hT2", [128, ND * TOKC], BF16)).ap()
        b2 = mk_ffn_bufs(nc, es2, g, "B")
        b2["xs3"] = xs2v
        b2["hT"] = hT2
        if "attn" in phases:
            kb.wait("pool", "s_wops", kb.n("s_wops"))
        emit_ffn(kb, b2, dram, call=1)
        kb.flush("ffn2")
        xs_final = xs2v
        es2.close()
    else:
        xs_final = xs1v if "attn" not in phases else kb._xs_wo

    kb.wait("sp", "s_ev", kb.n("s_ev"))
    kb.wait("sp", "s_woe", kb.n("s_woe"))
    kb.do("sp", "dma_start",
          dram["x_out"].rearrange("(tc p) d -> p tc d", p=128),
          xs_final, incs=[("s_out", 16)])
    nout = 16
    if DBG["dump"]:
        dbg_h = nc.dram_tensor("dbg_h", [128, ND * TOKC], F32, kind="ExternalOutput").ap()
        dbg_c = nc.dram_tensor("dbg_c", [64, 1024], F32, kind="ExternalOutput").ap()
        dbg_a = nc.dram_tensor("dbg_a", [128, 4096], F32, kind="ExternalOutput").ap()
        dbg_g = nc.dram_tensor("dbg_g", [128, 8], F32, kind="ExternalOutput").ap()
        dbg_w = nc.dram_tensor("dbg_w", [128, 24576], F32, kind="ExternalOutput").ap()
        dbg_s2 = nc.dram_tensor("dbg_s2", [128, 1024], F32, kind="ExternalOutput").ap()
        kb.wait("pool", "s_ev", kb.n("s_ev"))
        kb.do("pool", "dma_start", dbg_h[:], b1["hT"][:], incs=[("s_out", 16)])
        kb.do("pool", "dma_start", dbg_c[:], b1["combT"][:], incs=[("s_out", 16)])
        kb.do("pool", "dma_start", dbg_a[:], b1["aT"][0][:], incs=[("s_out", 16)])
        kb.do("pool", "dma_start", dbg_g[:], b1["gate_col"][:], incs=[("s_out", 16)])
        kb.do("pool", "dma_start", dbg_w[:], b1["arena_raw"][:, 0:24576], incs=[("s_out", 16)])
        kb.do("pool", "dma_start", dbg_s2[:], b1["shsg"], incs=[("s_out", 16)])
        nout += 96
    kb.wait("sp", "s_out", nout)
    kb.wait("pool", "s_out", nout)
    kb.flush("out")
    if "attn" in phases:
        kb._es_wo.close()
    return nc


def emit_attn(kb, g, dram):
    nc = kb.nc
    groups4 = [[0, 1, 2, 3], [4, 5, 6, 7]]

    esp = ExitStack()
    qT = esp.enter_context(nc.sbuf_tensor("a_qT", [128, NH * T], BF16)).ap()
    kT = esp.enter_context(nc.sbuf_tensor("a_kT", [128, NH * T], BF16)).ap()
    vA = esp.enter_context(nc.sbuf_tensor("a_v", [128, 32 * 512], BF16)).ap()
    oTn = esp.enter_context(nc.sbuf_tensor("a_oTn", [128, T], BF16)).ap()
    wqkv = esp.enter_context(nc.sbuf_tensor("a_wqkv", [128, ND * 1536], BF16)).ap()
    NSL = 20
    h2sl = [esp.enter_context(nc.sbuf_tensor(f"a_h2_{i}", [128, 256], BF16)).ap()
            for i in range(NSL)]
    fsc = esp.enter_context(nc.sbuf_tensor("a_fsc", [128, 2 * 512], F32)).ap()
    fbq = esp.enter_context(nc.sbuf_tensor("a_fbq", [128, 512], BF16)).ap()
    fbk = esp.enter_context(nc.sbuf_tensor("a_fbk", [128, 512], BF16)).ap()
    fcl = esp.enter_context(nc.sbuf_tensor("a_fcl", [128, 16], F32)).ap()
    ssq = [esp.enter_context(nc.sbuf_tensor(f"a_ssq{i}", [128, 512], BF16)).ap()
           for i in range(2)]
    ssqr = esp.enter_context(nc.sbuf_tensor("a_ssqr", [128, 512], BF16)).ap()
    zrA = esp.enter_context(nc.sbuf_tensor("a_zrA", [1, 512], F32)).ap()
    zrB = esp.enter_context(nc.sbuf_tensor("a_zrB", [1, 512], F32)).ap()
    zbc = esp.enter_context(nc.sbuf_tensor("a_zbc", [128, 512], F32)).ap()

    gbq = g["gbq"].rearrange("p (a f) -> p a f", f=512)
    masks = g["masks"].rearrange("p (m f) -> p m f", f=512)
    qT4 = qT.rearrange("p (h t) -> p h t", t=T)
    kT4 = kT.rearrange("p (h t) -> p h t", t=T)
    vA3 = vA.rearrange("p (tc f) -> p tc f", f=512)
    wqkv3 = wqkv.rearrange("p (d f) -> p d f", f=1536)

    es_ps1 = ExitStack()
    qkv_ps = [es_ps1.enter_context(nc.psum_tensor(f"a_qkv{i}", [128, 1536], F32)).ap()
              for i in range(2)]
    tr_ps = [es_ps1.enter_context(nc.psum_tensor(f"a_tr{i}", [128, 512], BF16)).ap()
             for i in range(2)]

    kb.do("pool", "dma_start", wqkv3,
          dram["wqkvT"].rearrange("(d p) f -> p d f", p=128),
          waits=[("s_spill", 16), ("s_ag1st", 16)], incs=[("s_wqkv", 16)])

    for tcb in range(16):
        rk, co = tcb // 4, (tcb % 4) * 256
        for d in range(ND):
            kk = tcb * ND + d
            w = [("s_spill", 16), ("s_cc1", kb.n("s_cc1"))]
            if kk >= NSL:
                w.append(("s_h2c", kk - NSL + 1))
            kb.do("sp", "dma_start", h2sl[kk % NSL][:],
                  dram["ag1_out"][rk * D + d * 128: rk * D + (d + 1) * 128,
                                  co:co + 256],
                  waits=w, incs=[(f"s_h2f{kk % NSL}", 16)])
        for tci in range(2):
            tc = tcb * 2 + tci
            pq = tc % 2
            for d in range(ND):
                kk = tcb * ND + d
                w = []
                if d == 0:
                    w = [("s_wqkv", 16)]
                    if tc >= 2:
                        w += [("s_ve", tc - 1), ("s_fm2", (tc - 2) * 2 + 2)]
                w2 = [(f"s_h2f{kk % NSL}", (kk // NSL + 1) * 16)]
                for reg in range(3):
                    last = (reg == 2 and d == ND - 1)
                    kb.do("pe", "matmul",
                          qkv_ps[pq][:, reg * 512:(reg + 1) * 512],
                          h2sl[kk % NSL][:, tci * 128:(tci + 1) * 128],
                          wqkv3[:, d, reg * 512:(reg + 1) * 512],
                          start=(d == 0), stop=(d == ND - 1),
                          waits=(w + w2) if reg == 0 else [],
                          incs=([("s_qkvp", 1)] if last else
                                ([("s_h2c", 1)] if tci == 1 and reg == 2 else [])))
                    if tci == 1 and reg == 2 and last:
                        kb.carry_pe("s_h2c", 1)
            qn = kb.n("s_qkvp")
            kb.do("dve", "tensor_copy", vA3[:, tc, :], qkv_ps[pq][:, 1024:1536],
                  waits=[("s_qkvp", qn)], incs=[("s_ve", 1)])
            for qk in range(2):
                base = qk * 512
                kb.do("dve", "scalar_tensor_tensor", fsc[:, 0:512],
                      qkv_ps[pq][:, base:base + 512], 1.0, gbq[:, 0, :],
                      op0=ALU.bypass, op1=ALU.mult,
                      waits=[("s_qkvp", qn), ("s_ftp", kb.n("s_ftp")),
                             ("s_fmB", kb.n("s_fmB"))],
                      incs=[("s_fm1", 1)])
                kb.do("dve", "tensor_tensor", fsc[:, 0:512], fsc[:, 0:512],
                      gbq[:, 1, :], op=ALU.add,
                      waits=[("s_fm1", kb.n("s_fm1"))], incs=[("s_fm2", 1)])
                kb.do("dve", "reduce_sum", fcl[:, 0:4],
                      fsc[:, 0:512].rearrange("p (h dd) -> p h dd", dd=128),
                      axis=AX.X,
                      waits=[("s_fm2", kb.n("s_fm2"))], incs=[("s_fm3", 1)])
                kb.do("act", "activation", fsc[:, 512:1024], fsc[:, 0:512],
                      AF.Square,
                      waits=[("s_fm2", kb.n("s_fm2")), ("s_fm5", kb.n("s_fm5"))],
                      incs=[("s_fm4", 1)])
                kb.do("dve", "reduce_sum", fcl[:, 4:8],
                      fsc[:, 512:1024].rearrange("p (h dd) -> p h dd", dd=128),
                      axis=AX.X,
                      waits=[("s_fm4", kb.n("s_fm4"))], incs=[("s_fm5", 1)])
                kb.do("dve", "tensor_scalar", fcl[:, 8:12], fcl[:, 0:4],
                      1.0 / 128, None, op0=ALU.mult,
                      waits=[("s_fm3", kb.n("s_fm3"))], incs=[("s_fm6", 1)])
                kb.do("dve", "tensor_tensor", fcl[:, 12:16], fcl[:, 8:12],
                      fcl[:, 8:12], op=ALU.mult,
                      waits=[("s_fm6", kb.n("s_fm6"))], incs=[("s_fm7", 1)])
                kb.do("dve", "scalar_tensor_tensor", fcl[:, 0:4], fcl[:, 4:8],
                      1.0 / 128, fcl[:, 12:16], op0=ALU.mult, op1=ALU.subtract,
                      waits=[("s_fm5", kb.n("s_fm5")), ("s_fm7", kb.n("s_fm7"))],
                      incs=[("s_fm8", 1)])
                scale = 128.0 if qk == 0 else 1.0
                kb.do("act", "activation", fcl[:, 4:8], fcl[:, 0:4], AF.Sqrt,
                      bias=scale * EPS, scale=scale,
                      waits=[("s_fm8", kb.n("s_fm8"))], incs=[("s_fm9", 1)])
                kb.do("dve", "reciprocal", fcl[:, 12:16], fcl[:, 4:8],
                      waits=[("s_fm9", kb.n("s_fm9"))], incs=[("s_fmA", 1)])
                dstb = fbq if qk == 0 else fbk
                for h in range(NH):
                    kb.do("dve", "tensor_scalar", dstb[:, h * 128:(h + 1) * 128],
                          fsc[:, h * 128:(h + 1) * 128], fcl[:, 8 + h:9 + h],
                          fcl[:, 12 + h:13 + h], op0=ALU.subtract, op1=ALU.mult,
                          waits=[("s_fmA", kb.n("s_fmA"))],
                          incs=([("s_fmB", 1)] if h == NH - 1 else []))
            for qk in range(2):
                srcb = fbq if qk == 0 else fbk
                for h in range(NH):
                    w = []
                    if h == 0:
                        w = [("s_fmB", kb.n("s_fmB")),
                             ("s_fme", max(0, kb.n("s_fme") - 1))]
                    kb.do("pe", "transpose", tr_ps[qk][:, h * 128:(h + 1) * 128],
                          srcb[:, h * 128:(h + 1) * 128], g["ident_bf"][:],
                          waits=w, incs=([("s_ftp", 1)] if h == NH - 1 else []))
                dst = qT4 if qk == 0 else kT4
                kb.do("dve", "tensor_copy",
                      dst[:, :, tc * 128:(tc + 1) * 128],
                      tr_ps[qk].rearrange("p (h t) -> p h t", t=128),
                      waits=[("s_ftp", kb.n("s_ftp"))], incs=[("s_fme", 1)])
    kb.flush("attn_proj")
    es_ps1.close()

    es_ps2 = ExitStack()
    s_ps = [es_ps2.enter_context(nc.psum_tensor(f"a_s{i}", [128, 512], F32)).ap()
            for i in range(2)]
    o_ps = [es_ps2.enter_context(nc.psum_tensor(f"a_o{i}", [128, 512], F32)).ap()
            for i in range(2)]
    z_ps = es_ps2.enter_context(nc.psum_tensor("a_z", [1, 512], F32)).ap()
    zb_ps = es_ps2.enter_context(nc.psum_tensor("a_zb", [128, 512], F32)).ap()

    fme_total = kb.n("s_fme")
    ve_total = kb.n("s_ve")
    for h in range(NH):
        for qt in range(8):
            idx = h * 8 + qt
            nkb = 4 * qt + 4
            base_S = kb.n("s_S")
            base_sq = kb.n("s_sq")
            base_sqa = kb.n("s_sqa")
            base_oz = kb.n("s_oz")

            sq_done = {}

            def emit_S_sq(kbi):
                w = []
                if kbi == 0:
                    w = [("s_fme", min(fme_total, 2 * (4 * qt + 4))),
                         ("s_ve", min(ve_total, 4 * qt + 4))]
                if kbi >= 2:
                    w.append(sq_done[kbi - 2])
                kb.do("pe", "matmul", s_ps[kbi % 2][:],
                      kT4[:, h, kbi * 128:(kbi + 1) * 128],
                      qT4[:, h, qt * 512:(qt + 1) * 512],
                      start=True, stop=True, waits=w, incs=[("s_S", 1)])
                diag = kbi >= 4 * qt
                tgt = ssqr if diag else ssq[kbi % 2]
                wsq = [("s_S", base_S + kbi + 1)]
                if kbi >= 2:
                    wsq.append(("s_oz", base_oz + kbi - 1))
                if diag:
                    wsq.append(("s_sq", kb.n("s_sq")))
                kb.do("act", "activation", tgt[:], s_ps[kbi % 2][:], AF.Square,
                      waits=wsq,
                      incs=[("s_sqa", 1)] if diag else [("s_sq", 1)])
                sq_done[kbi] = ("s_sqa" if diag else "s_sq",
                                kb.n("s_sqa") if diag else kb.n("s_sq"))
                if diag:
                    kb.do("dve", "tensor_tensor", ssq[kbi % 2][:], ssqr[:],
                          masks[:, kbi - 4 * qt, :], op=ALU.mult,
                          waits=[("s_sqa", kb.n("s_sqa"))],
                          incs=[("s_sq", 1)])

            def emit_OZ(kbi):
                lastk = (kbi == nkb - 1)
                w = [("s_sq", base_sq + kbi + 1)]
                if kbi == 0:
                    w += [("s_on", max(0, idx - 1)), ("s_z", 2 * idx)]
                kb.do("pe", "matmul", o_ps[qt % 2][:],
                      vA3[:, kbi, h * 128:(h + 1) * 128], ssq[kbi % 2][:],
                      start=(kbi == 0), stop=lastk, waits=w)
                kb.do("pe", "matmul", z_ps[:], g["ones_bf"][:],
                      ssq[kbi % 2][:], start=(kbi == 0), stop=lastk,
                      incs=[("s_oz", 1)])

            for kbi in range(nkb):
                emit_S_sq(kbi)
                if kbi >= 1:
                    emit_OZ(kbi - 1)
            emit_OZ(nkb - 1)

            kb.do("act", "activation", zrA[:], z_ps[:], AF.Ln, bias=EPS,
                  waits=[("s_oz", base_oz + nkb), ("s_zbMM", kb.n("s_zbMM"))],
                  incs=[("s_z", 1)])
            kb.do("act", "activation", zrB[:], zrA[:], AF.Exp,
                  scale=-1.0,
                  waits=[("s_z", kb.n("s_z"))], incs=[("s_z", 1)])
            kb.do("pe", "matmul", zb_ps[:], g["ones_row"][:], zrB[:],
                  start=True, stop=True,
                  waits=[("s_z", kb.n("s_z")),
                         ("s_zbe", max(0, kb.n("s_zbe") - 1))],
                  incs=[("s_zbMM", 1)])
            kb.do("dve", "tensor_copy", zbc[:], zb_ps[:],
                  waits=[("s_zbMM", kb.n("s_zbMM"))], incs=[("s_zbe", 1)])
            kb.do("dve", "tensor_tensor", oTn[:, qt * 512:(qt + 1) * 512],
                  o_ps[qt % 2][:], zbc[:], op=ALU.mult,
                  waits=[("s_zbe", kb.n("s_zbe")), ("s_oz", kb.n("s_oz")),
                         ("s_ag2st", 16 * h)],
                  incs=[("s_on", 1)])
        kb.wait("sp", "s_on", kb.n("s_on"))
        kb.do("sp", "dma_start",
              dram["ag2_in"].rearrange("(tb q p) t -> p tb q t", q=4, p=128)
              [:, :, h, :],
              oTn.rearrange("p (tb t) -> p tb t", t=1024),
              incs=[("s_ag2st", 16)])
    kb.coll("AllGather", groups4, dram["ag2_in"], dram["ag2_out"],
            waits=[("s_ag2st", kb.n("s_ag2st"))], incs=[("s_cc2", 1)])
    soz_Z = kb.n("s_oz")
    soz_on = kb.n("s_on")
    kb.flush("attn_soz")
    es_ps2.close()
    esp.close()

    es_wo = ExitStack()
    es_wo2 = ExitStack()
    xs2 = es_wo.enter_context(nc.sbuf_tensor("xs2", [128, NTC * D], F32)).ap()
    xs2v = xs2.rearrange("p (tc d) -> p tc d", d=D)
    oTf = es_wo2.enter_context(nc.sbuf_tensor("w_oTf", [128, ND * 1024], BF16)).ap()
    woT = es_wo2.enter_context(nc.sbuf_tensor("w_woT", [128, ND * D], BF16)).ap()
    wo_ps = [es_wo2.enter_context(nc.psum_tensor(f"w_wo{i}", [128, D], F32)).ap()
             for i in range(2)]
    oTf3 = oTf.rearrange("p (d t) -> p d t", t=1024)
    woT3 = woT.rearrange("p (d f) -> p d f", f=D)

    region = [("s_oz", soz_Z), ("s_on", soz_on)]
    kb.do("pool", "dma_start", woT3,
          dram["woT"].rearrange("(d p) f -> p d f", p=128),
          waits=region, incs=[("s_woT", 16)])
    for ch in range(ND):
        kb.do("pool", "indirect_dma_start",
              out=oTf3[:, ch, :], out_offset=None,
              in_=dram["ag2_out"][:],
              in_offset=bass.IndirectOffsetOnAxis(ap=g["gthidx"][:, ch:ch + 1],
                                                  axis=0),
              waits=([("s_cc2", kb.n("s_cc2"))] + region if ch == 0 else []),
              incs=[("s_gth", 16)])
    kb.do("sp", "dma_start", xs2v,
          dram["xspill"].rearrange("(tc p) d -> p tc d", p=128),
          waits=region, incs=[("s_xsr", 16)])
    for tc in range(NTC):
        w = [("s_gth", 16 * ND), ("s_woT", 16)]
        if tc >= 2:
            w.append(("s_woe", tc - 1))
        for hd in range(ND):
            for on in range(4):
                last = (hd == ND - 1 and on == 3)
                kb.do("pe", "matmul",
                      wo_ps[tc % 2][:, on * 512:(on + 1) * 512],
                      oTf3[:, hd, tc * 128:(tc + 1) * 128],
                      woT3[:, hd, on * 512:(on + 1) * 512],
                      start=(hd == 0), stop=(hd == ND - 1),
                      waits=w if (hd == 0 and on == 0) else [],
                      incs=([("s_wops", 1)] if last else []))
        kb.do("dve", "tensor_tensor", xs2v[:, tc, :], xs2v[:, tc, :],
              wo_ps[tc % 2][:], op=ALU.add,
              waits=[("s_wops", kb.n("s_wops")), ("s_xsr", 16)],
              incs=[("s_woe", 1)])
    kb.flush("wo")
    es_wo2.close()
    kb._xs_wo = xs2v
    kb._es_wo = es_wo


def make_in_maps(L0_attn_hidden_states, L0_attn_Wq, L0_attn_Wk, L0_attn_Wv,
                 L0_attn_Wo, L0_attn_gamma, L0_attn_beta, L0_ffn_router_w,
                 L0_ffn_gate_up_proj, L0_ffn_down_proj, L0_ffn_shared_gate_w,
                 L0_ffn_shared_up_w, L0_ffn_shared_down_w,
                 L0_ffn_shared_expert_gate_w):
    x = np.ascontiguousarray(np.asarray(L0_attn_hidden_states, np.float32).reshape(8192, D))
    wguT = np.ascontiguousarray(np.asarray(L0_ffn_gate_up_proj, np.float32).transpose(0, 2, 1))
    wdT = np.ascontiguousarray(np.asarray(L0_ffn_down_proj, np.float32).transpose(0, 2, 1))
    wrT = np.ascontiguousarray(np.asarray(L0_ffn_router_w, np.float32).T)
    wsgT = np.ascontiguousarray(np.asarray(L0_ffn_shared_gate_w, np.float32).T)
    wsuT = np.ascontiguousarray(np.asarray(L0_ffn_shared_up_w, np.float32).T)
    wsdT = np.ascontiguousarray(np.asarray(L0_ffn_shared_down_w, np.float32).T)
    _wseg_col = np.asarray(L0_ffn_shared_expert_gate_w, np.float32).reshape(16, 128)
    wseg8 = np.ascontiguousarray(
        np.broadcast_to(_wseg_col.T[:, :, None], (128, 16, 8)).reshape(128, 128))
    WqT = np.ascontiguousarray(np.asarray(L0_attn_Wq, np.float32).T)
    WkT = np.ascontiguousarray(np.asarray(L0_attn_Wk, np.float32).T)
    WvT = np.ascontiguousarray(np.asarray(L0_attn_Wv, np.float32).T)
    WoT = np.ascontiguousarray(np.asarray(L0_attn_Wo, np.float32).T)
    gamma = np.asarray(L0_attn_gamma, np.float32)
    beta = np.asarray(L0_attn_beta, np.float32)

    ident_bf = np.eye(128).astype(BF)
    ident_f32 = np.eye(128, dtype=np.float32)
    masks = np.zeros((4, 128, 512), BF)
    for c in range(4):
        for p in range(128):
            masks[c, p, c * 128 + p:] = 1.0
    ones_bf = np.ones((128, 1), BF)
    ones_row = np.ones((1, 128), np.float32)
    onesone = np.ones((1, 1), np.float32)
    fconst = np.ascontiguousarray(
        np.broadcast_to(np.array([EPS, 128 * EPS], np.float32), (128, 2)))
    gbq = np.ascontiguousarray(np.stack(
        [np.broadcast_to(np.tile(gamma, NH), (128, 512)),
         np.broadcast_to(np.tile(beta, NH), (128, 512))]).astype(np.float32))

    in_maps = []
    for c in range(8):
        r = c % 4
        cols = slice(512 * r, 512 * (r + 1))
        wqkvT = np.ascontiguousarray(
            np.concatenate([WqT[:, cols], WkT[:, cols], WvT[:, cols]], axis=1))
        idx = np.zeros((128, 16), np.uint32)
        for p in range(128):
            for ch in range(16):
                hd = ch * 128 + p
                idx[p, ch] = (hd // 512) * 2048 + r * 512 + (hd % 512)
        in_maps.append({
            "x_in": np.ascontiguousarray(x[c * TOKC:(c + 1) * TOKC]),
            "wguT": wguT, "wdT": wdT, "wrT": wrT, "wsgT": wsgT, "wsuT": wsuT,
            "wsdT": wsdT, "wseg8": wseg8, "wqkvT": wqkvT, "woT": WoT,
            "gbq": gbq, "ident_bf": ident_bf, "ident_f32": ident_f32,
            "masks": masks, "ones_bf": ones_bf, "ones_row": ones_row,
            "onesone": onesone, "gthidx": idx,
            "fconst": fconst,
        })
    return in_maps


def kernel(**inputs):
    in_maps = make_in_maps(**inputs)
    nc = build_nc()
    res = run_bass_kernel_spmd(nc, in_maps, core_ids=list(range(8)))
    out = np.concatenate([np.asarray(res.results[c]["x_out"]) for c in range(8)],
                         axis=0)
    return np.ascontiguousarray(out.reshape(2, 4096, D).astype(np.float32))
